# version 02
# baseline (speedup 1.0000x reference)
import numpy as np
import ml_dtypes
from contextlib import ExitStack

import concourse.bass as bass
import concourse.mybir as mybir
from concourse.bass_utils import run_bass_kernel_spmd

F32 = mybir.dt.float32
BF16 = mybir.dt.bfloat16
AF = mybir.ActivationFunctionType
ALU = mybir.AluOpType
AX = mybir.AxisListType
NPBF16 = ml_dtypes.bfloat16

B, S, H, DI, NST, R, FF = 2, 8192, 2048, 4096, 16, 128, 5632
T = 2048
HALO = 3
T1 = T + HALO
NCORES = 8
EPS_LN = 1e-5
EPS_RMS = 1e-6
HT = H // 128
DIT = DI // 128
FFT = FF // 128

ENGINES = ("sync", "gpsimd", "vector", "scalar", "tensor")
import os as _os
DBG_SERIAL = bool(_os.environ.get("DBG_SERIAL"))
DBG_CLS = set((_os.environ.get("DBG_CLS") or "").split(","))

CONST_VALS = [0.0, 1.0, -1.0, 0.5, -0.5, 1.0 / H, 1.0 / R, 1.0 / NST,
              EPS_LN, EPS_RMS]


def chunks_of(total, w=512):
    out = []
    o = 0
    while o < total:
        c = min(w, total - o)
        out.append((o, c))
        o += c
    return out


class Sem:
    __slots__ = ("h", "v")

    def __init__(self, h):
        self.h = h
        self.v = 0


class Stage:
    def __init__(self, prog, fence=True):
        self.p = prog
        self.q = {e: [] for e in ENGINES}
        self.fence = [(s.h, s.v) for s in prog.all_sems] if fence else []

    def on(self, eng, fn):
        self.q[eng].append(fn)

    def flush(self):
        for eng in ENGINES:
            fns = self.q[eng]
            if not fns:
                continue
            fence = self.fence

            def body(E, fns=fns, fence=fence):
                for h, v in fence:
                    if v > 0:
                        E.wait_ge(h, v)
                for f in fns:
                    f(E)

            getattr(self.p.block, eng)(body)
        self.q = {e: [] for e in ENGINES}


class Prog:
    def __init__(self, nc, ctx, block):
        self.nc = nc
        self.ctx = ctx
        self._stack = ctx
        self.block = block
        self.LD = self.sem("sLD")
        self.GD = self.sem("sGD")
        self.MM = self.sem("sMM")
        self.VE = self.sem("sVE")
        self.SC = self.sem("sSC")
        self.PL = self.sem("sPL")
        self.all_sems = [self.LD, self.GD, self.MM, self.VE, self.SC, self.PL]
        self._pool = {}
        self.CONST = self.sb("CONSTS", [128, len(CONST_VALS)], F32)
        self._cidx = {v: i for i, v in enumerate(CONST_VALS)}

    def psem(self, name):
        if name not in self._pool:
            s = self.sem("d_" + name)
            self._pool[name] = s
            self.all_sems.append(s)
        return self._pool[name]

    def sem(self, name):
        return Sem(self.ctx.enter_context(self.nc.semaphore(name)))

    def sb(self, name, shape, dt=F32):
        return self._stack.enter_context(self.nc.sbuf_tensor(name, shape, dt))

    def ps(self, name, shape, dt=F32):
        return self._stack.enter_context(self.nc.psum_tensor(name, shape, dt))

    from contextlib import contextmanager as _cm2

    @_cm2
    def scope(self):
        old = self._stack
        es = ExitStack()
        self._stack = es
        try:
            yield
        finally:
            es.close()
            self._stack = old

    def cc(self, val):
        i = self._cidx[val]
        return self.CONST[:, i:i + 1]

    def cr(self, val):
        i = self._cidx[val]
        return self.CONST[0:1, i:i + 1]

    def crn(self, val, n):
        i = self._cidx[val]
        return self.CONST[0:n, i:i + 1]

    def emit_const_init(self):
        st = Stage(self, fence=False)

        def f(E):
            for v, i in self._cidx.items():
                ins = E.memset(self.CONST[:, i:i + 1], float(v))
            ins.then_inc(self.PL.h, 1)

        self.PL.v += 1
        st.on("gpsimd", f)
        st.flush()

    def dma(self, st, eng, sem, out, in_, waits=(), cls=None, **kw):
        if DBG_SERIAL or (cls is not None and cls in DBG_CLS):
            waits = [(s.h, s.v) for s in self.all_sems]
        sem.v += 16
        waits = tuple(waits)

        def f(E, out=out, in_=in_, kw=kw, sem=sem):
            for h, v in waits:
                E.wait_ge(h, v)
            E.dma_start(out=out, in_=in_, **kw).then_inc(sem.h, 16)

        st.on(eng, f)
        return sem.v

    def allw(self):
        return [(s.h, s.v) for s in self.all_sems]

    def op(self, st, eng, sem, build, waits=(), cls=None):
        if DBG_SERIAL or (cls is not None and cls in DBG_CLS):
            waits = [(s.h, s.v) for s in self.all_sems]
        if sem is not None:
            sem.v += 1
        waits = tuple(waits)

        def f(E, build=build, sem=sem):
            for h, v in waits:
                E.wait_ge(h, v)
            ins = build(E)
            if sem is not None and ins is not None:
                ins.then_inc(sem.h, 1)

        st.on(eng, f)
        return sem.v if sem is not None else None

    def wait(self, st, eng, waits):
        waits = tuple(waits)

        def f(E):
            for h, v in waits:
                E.wait_ge(h, v)

        st.on(eng, f)


def init_consts(p, selB_dram=None, selBC2_dram=None):
    p.emit_const_init()
    st = Stage(p, fence=False)
    c = {}
    c["ones_col_b"] = p.sb("ones_col_b", [128, 1], BF16)
    c["ones_row_f"] = p.sb("ones_row_f", [1, 128], F32)

    def f(E):
        E.memset(c["ones_col_b"][:], 1.0)
        E.memset(c["ones_row_f"][:], 1.0).then_inc(p.PL.h, 1)

    p.PL.v += 1
    st.on("gpsimd", f)
    if selB_dram is not None:
        c["selB_b"] = p.sb("selB_b", [32, 2], BF16)
        c["selBC2_f"] = p.sb("selBC2_f", [2, 32], F32)
        p.dma(st, "sync", p.LD, c["selB_b"][:], selB_dram)
        p.dma(st, "sync", p.LD, c["selBC2_f"][:], selBC2_dram)
    st.flush()
    return c



def ln_sweep(p, consts, kind, in_dram, out_dram, gain_cm_dram, Tt, name,
             out2_dram=None):
    st = Stage(p)
    XS = [p.sb(f"{name}_x{k}", [128, 512], F32) for k in range(HT)]
    XB = p.sb(f"{name}_xb", [128, 512], BF16)
    XQ = p.sb(f"{name}_xq", [128, 512], BF16)
    ZB = [p.sb(f"{name}_zb{i}", [128, 512], BF16) for i in range(2)]
    Z2 = [p.sb(f"{name}_z2{i}", [128, 512], F32) for i in range(2)] \
        if out2_dram is not None else None
    so2 = [p.psem("ln_p0"), p.psem("ln_p1")] if out2_dram is not None else None
    z2_gd = [0, 0]
    RW = [p.sb(f"{name}_rw{i}", [1, 512], F32) for i in range(5)]
    GCOL = None
    sx = p.psem("ln_x")
    so = [p.psem("ln_o0"), p.psem("ln_o1")]
    sm = p.psem("misc")
    misc_lv = 0
    if gain_cm_dram is not None:
        GCOL = p.sb(f"{name}_g", [128, HT], F32)
        misc_lv = p.dma(st, "sync", sm, GCOL[:], gain_cm_dram)
    s1_ps = p.ps(f"{name}_s1", [1, 512], F32)
    s2_ps = p.ps(f"{name}_s2", [1, 512], F32)
    mub_ps = p.ps(f"{name}_mub", [128, 512], F32)
    rhob_ps = p.ps(f"{name}_rhob", [128, 512], F32)
    r1b_ps = p.ps(f"{name}_r1b", [128, 512], F32) if out2_dram is not None else None

    last_pass2_ve = 0
    zb_gd = [0, 0]
    for ci, (o, w) in enumerate(chunks_of(Tt)):
        for kk in range(HT):
            p.dma(st, "sync", sx, XS[kk][:, 0:w],
                  in_dram[kk * 128:(kk + 1) * 128, o:o + w],
                  waits=[(p.VE.h, last_pass2_ve)])
        lv_all = sx.v
        for kk in range(HT):
            lv = lv_all
            prev_mm = p.MM.v
            cv = p.op(st, "vector", p.VE,
                      lambda E, kk=kk, w=w: E.tensor_copy(XB[:, 0:w], XS[kk][:, 0:w]),
                      waits=[(sx.h, lv), (p.MM.h, prev_mm)])
            mm1 = p.op(st, "tensor", p.MM,
                       lambda E, kk=kk, w=w: E.matmul(
                           s1_ps[:, 0:w], consts["ones_col_b"][:], XB[:, 0:w],
                           start=(kk == 0), stop=(kk == HT - 1), skip_group_check=True),
                       waits=[(p.VE.h, cv)])
            sv = p.op(st, "scalar", p.SC,
                      lambda E, kk=kk, w=w: E.square(XQ[:, 0:w], XS[kk][:, 0:w]),
                      waits=[(sx.h, lv), (p.MM.h, mm1)])
            p.op(st, "tensor", p.MM,
                 lambda E, kk=kk, w=w: E.matmul(
                     s2_ps[:, 0:w], consts["ones_col_b"][:], XQ[:, 0:w],
                     start=(kk == 0), stop=(kk == HT - 1), skip_group_check=True),
                 waits=[(p.SC.h, sv)])
        stats_done = p.MM.v

        eps0 = EPS_LN if kind in ("ln", "lnrms") else EPS_RMS
        if kind in ("ln", "lnrms"):
            sv = p.op(st, "scalar", p.SC,
                      lambda E, w=w: E.activation(RW[0][:, 0:w], s1_ps[:, 0:w],
                                                  AF.Copy, scale=p.cr(1.0 / H)),
                      waits=[(p.MM.h, stats_done), (sm.h, misc_lv)])
            sv = p.op(st, "scalar", p.SC,
                      lambda E, w=w: E.square(RW[2][:, 0:w], RW[0][:, 0:w]),
                      waits=[(p.SC.h, sv)])
            sv = p.op(st, "scalar", p.SC,
                      lambda E, w=w: E.activation(RW[3][:, 0:w], s2_ps[:, 0:w],
                                                  AF.Copy, scale=p.cr(1.0 / H)))
            vv = p.op(st, "vector", p.VE,
                      lambda E, w=w: E.tensor_sub(RW[2][:, 0:w], RW[3][:, 0:w],
                                                  RW[2][:, 0:w]),
                      waits=[(p.SC.h, sv)])
        else:
            sv = p.op(st, "scalar", p.SC,
                      lambda E, w=w: E.activation(RW[3][:, 0:w], s2_ps[:, 0:w],
                                                  AF.Copy, scale=p.cr(1.0 / H)),
                      waits=[(p.MM.h, stats_done), (sm.h, misc_lv)])
            vv = p.op(st, "vector", p.VE,
                      lambda E, w=w: E.tensor_copy(RW[2][:, 0:w], RW[3][:, 0:w]),
                      waits=[(p.SC.h, sv)])
        sv = p.op(st, "scalar", p.SC,
                  lambda E, w=w: E.activation(RW[3][:, 0:w], RW[2][:, 0:w],
                                              AF.Ln, bias=p.cr(eps0)),
                  waits=[(p.VE.h, vv)])
        sv = p.op(st, "scalar", p.SC,
                  lambda E, w=w: E.activation(RW[1][:, 0:w], RW[3][:, 0:w],
                                              AF.Exp, scale=p.cr(-0.5)),
                  waits=[(p.SC.h, sv)])
        if kind == "lnrms":
            if out2_dram is not None:
                sv = p.op(st, "scalar", p.SC,
                          lambda E, w=w: E.copy(RW[4][:, 0:w], RW[1][:, 0:w]),
                          waits=[(p.SC.h, sv)])
            vv = p.op(st, "vector", p.VE,
                      lambda E, w=w: E.tensor_mul(RW[3][:, 0:w], RW[1][:, 0:w],
                                                  RW[1][:, 0:w]),
                      waits=[(p.SC.h, sv)])
            vv = p.op(st, "vector", p.VE,
                      lambda E, w=w: E.tensor_mul(RW[3][:, 0:w], RW[3][:, 0:w],
                                                  RW[2][:, 0:w]),
                      waits=[(p.VE.h, vv)])
            sv = p.op(st, "scalar", p.SC,
                      lambda E, w=w: E.activation(RW[3][:, 0:w], RW[3][:, 0:w],
                                                  AF.Ln, bias=p.cr(EPS_RMS)),
                      waits=[(p.VE.h, vv)])
            sv = p.op(st, "scalar", p.SC,
                      lambda E, w=w: E.activation(RW[2][:, 0:w], RW[3][:, 0:w],
                                                  AF.Exp, scale=p.cr(-0.5)),
                      waits=[(p.SC.h, sv)])
            vv = p.op(st, "vector", p.VE,
                      lambda E, w=w: E.tensor_mul(RW[1][:, 0:w], RW[1][:, 0:w],
                                                  RW[2][:, 0:w]),
                      waits=[(p.SC.h, sv)])

        row_done_sc, row_done_ve = p.SC.v, p.VE.v
        if kind in ("ln", "lnrms"):
            p.op(st, "tensor", p.MM,
                 lambda E, w=w: E.matmul(mub_ps[:, 0:w], consts["ones_row_f"][:],
                                         RW[0][:, 0:w], start=True, stop=True),
                 waits=[(p.SC.h, row_done_sc), (p.VE.h, row_done_ve)])
        brho = p.op(st, "tensor", p.MM,
                    lambda E, w=w: E.matmul(rhob_ps[:, 0:w], consts["ones_row_f"][:],
                                            RW[1][:, 0:w], start=True, stop=True),
                    waits=[(p.SC.h, row_done_sc), (p.VE.h, row_done_ve)])
        if out2_dram is not None:
            brho = p.op(st, "tensor", p.MM,
                        lambda E, w=w: E.matmul(r1b_ps[:, 0:w],
                                                consts["ones_row_f"][:],
                                                RW[4][:, 0:w], start=True, stop=True))

        for kk in range(HT):
            zb = ZB[kk % 2]
            wz = [(p.MM.h, brho), (so[kk % 2].h, zb_gd[kk % 2])]
            if kind in ("ln", "lnrms"):
                p.op(st, "vector", p.VE,
                     lambda E, kk=kk, w=w: E.tensor_sub(XS[kk][:, 0:w], XS[kk][:, 0:w], mub_ps[:, 0:w]),
                     waits=[(p.MM.h, brho)])
                if kind == "ln":
                    vv = p.op(st, "vector", p.VE,
                              lambda E, kk=kk, w=w, zb=zb: E.tensor_mul(
                                  zb[:, 0:w], XS[kk][:, 0:w], rhob_ps[:, 0:w]),
                              waits=wz)
                else:
                    vv = p.op(st, "vector", p.VE,
                              lambda E, kk=kk, w=w, zb=zb: E.scalar_tensor_tensor(
                                  zb[:, 0:w], XS[kk][:, 0:w], GCOL[:, kk:kk + 1],
                                  rhob_ps[:, 0:w], ALU.mult, ALU.mult),
                              waits=wz)
                    if out2_dram is not None:
                        z2 = Z2[kk % 2]
                        vv2 = p.op(st, "vector", p.VE,
                                   lambda E, kk=kk, w=w, z2=z2: E.tensor_mul(
                                       z2[:, 0:w], XS[kk][:, 0:w], r1b_ps[:, 0:w]),
                                   waits=[(so2[kk % 2].h, z2_gd[kk % 2])])
                        z2_gd[kk % 2] = p.dma(
                            st, "gpsimd", so2[kk % 2],
                            out2_dram[kk * 128:(kk + 1) * 128, o:o + w],
                            z2[:, 0:w], waits=[(p.VE.h, vv2)])
            else:
                vv = p.op(st, "vector", p.VE,
                          lambda E, kk=kk, w=w, zb=zb: E.scalar_tensor_tensor(
                              zb[:, 0:w], XS[kk][:, 0:w], GCOL[:, kk:kk + 1],
                              rhob_ps[:, 0:w], ALU.mult, ALU.mult),
                          waits=wz)
            zb_gd[kk % 2] = p.dma(st, "gpsimd", so[kk % 2],
                                  out_dram[kk * 128:(kk + 1) * 128, o:o + w],
                                  zb[:, 0:w], waits=[(p.VE.h, vv)])
        last_pass2_ve = p.VE.v
    st.flush()



def gu_sweep(p, consts, in_dram_b, wA_dram, wB_dram, sA_cm, sB_cm, out_dram_b,
             Tt, name):
    st = Stage(p)
    sin = p.psem("gu_in")
    swf = [p.psem("gu_w0"), p.psem("gu_w1")]
    sou = [p.psem("gu_o0"), p.psem("gu_o1")]
    sm = p.psem("misc")
    INB = [p.sb(f"{name}_in{k}", [128, Tt], BF16) for k in range(HT)]
    for k in range(HT):
        p.dma(st, "sync", sin, INB[k][:], in_dram_b[k * 128:(k + 1) * 128, :])
    in_all = sin.v
    MG = 4
    ngroups = (FFT + MG - 1) // MG
    WF = [p.sb(f"{name}_wf{i}", [128, MG * 128], F32) for i in range(2)]
    WS = [[[p.sb(f"{name}_ws{pp}_{m}_{k}", [128, MG * 128], BF16)
            for k in range(HT)] for m in range(2)] for pp in range(2)]
    SCOL = None
    misc_lv = 0
    if sA_cm is not None:
        SCOL = p.sb(f"{name}_scol", [128, 2 * FFT], F32)
        p.dma(st, "sync", sm, SCOL[:, 0:FFT], sA_cm)
        misc_lv = p.dma(st, "sync", sm, SCOL[:, FFT:2 * FFT], sB_cm)
    SGB = [p.sb(f"{name}_sg{i}", [128, 1024], BF16) for i in range(2)]
    OUTB = [p.sb(f"{name}_ob{i}", [128, 1024], BF16) for i in range(2)]
    psA = [p.ps(f"{name}_psA{i}", [128, 512], F32) for i in range(4)]
    psB = [p.ps(f"{name}_psB{i}", [128, 512], F32) for i in range(4)]

    tcs = chunks_of(Tt)
    phases = [tuple(range(i, min(i + 2, len(tcs)))) for i in range(0, len(tcs), 2)]
    wdr = [wA_dram, wB_dram]
    cast_done = [0, 0]
    mm_after_mg = []
    bank_free_ve = {}
    bank_free_sc = {}
    out_gd = [0, 0]

    for mg in range(ngroups):
        pp = mg % 2
        msz = min(MG, FFT - mg * MG)
        prior_mm = mm_after_mg[mg - 2] if mg >= 2 else 0
        for mat in range(2):
            for kk in range(HT):
                wsl = (mat * HT + kk) % 2
                wf = WF[wsl]
                lv = p.dma(st, "sync", swf[wsl], wf[:, 0:msz * 128],
                           wdr[mat][kk * 128:(kk + 1) * 128,
                                    mg * MG * 128:mg * MG * 128 + msz * 128],
                           waits=[(p.PL.h, max(0, p.PL.v - 1))])
                p.op(st, "gpsimd", p.PL,
                     lambda E, wf=wf, pp=pp, mat=mat, kk=kk, msz=msz:
                     E.tensor_copy(WS[pp][mat][kk][:, 0:msz * 128], wf[:, 0:msz * 128]),
                     waits=[(swf[wsl].h, lv), (p.MM.h, prior_mm)])
        cast_done[pp] = p.PL.v

        for mi in range(msz):
            m = mg * MG + mi
            for ph_i, ph in enumerate(phases):
                slot = ph_i % 2
                nb = len(ph)
                waits_mm = [(p.PL.h, cast_done[pp]),
                            (sin.h, in_all),
                            (p.VE.h, bank_free_ve.get(slot, 0)),
                            (p.SC.h, bank_free_sc.get(slot, 0))]
                for mat in range(2):
                    psl = (psA, psB)[mat]
                    for kk in range(HT):
                        for j, tci in enumerate(ph):
                            o, w = tcs[tci]
                            p.op(st, "tensor", p.MM,
                                 lambda E, mat=mat, kk=kk, j=j, o=o, w=w, slot=slot,
                                 pp=pp, mi=mi, psl=psl:
                                 E.matmul(psl[2 * slot + j][:, 0:w],
                                          WS[pp][mat][kk][:, mi * 128:(mi + 1) * 128],
                                          INB[kk][:, o:o + w],
                                          start=(kk == 0), stop=(kk == HT - 1),
                                          skip_group_check=True),
                                 waits=waits_mm if (kk == 0 and j == 0) else ())
                mmv = p.MM.v
                ob = OUTB[slot]
                sg = SGB[slot]
                for j, tci in enumerate(ph):
                    o, w = tcs[tci]
                    if sA_cm is not None:
                        sv = p.op(st, "scalar", p.SC,
                                  lambda E, j=j, o=o, w=w, slot=slot, m=m, sg=sg:
                                  E.activation(sg[:, j * 512:j * 512 + w],
                                               psA[2 * slot + j][:, 0:w], AF.Silu,
                                               scale=SCOL[:, m:m + 1]),
                                  waits=[(p.MM.h, mmv), (sm.h, misc_lv),
                                         (p.VE.h, bank_free_ve.get(slot, 0))])
                        vv = p.op(st, "vector", p.VE,
                                  lambda E, j=j, o=o, w=w, slot=slot, m=m, ob=ob, sg=sg:
                                  E.scalar_tensor_tensor(
                                      ob[:, j * 512:j * 512 + w], psB[2 * slot + j][:, 0:w],
                                      SCOL[:, FFT + m:FFT + m + 1], sg[:, j * 512:j * 512 + w],
                                      ALU.mult, ALU.mult),
                                  waits=[(p.SC.h, sv), (sou[slot].h, out_gd[slot])])
                    else:
                        sv = p.op(st, "scalar", p.SC,
                                  lambda E, j=j, o=o, w=w, slot=slot, sg=sg:
                                  E.activation(sg[:, j * 512:j * 512 + w],
                                               psA[2 * slot + j][:, 0:w], AF.Silu),
                                  waits=[(p.MM.h, mmv), (p.VE.h, bank_free_ve.get(slot, 0))])
                        vv = p.op(st, "vector", p.VE,
                                  lambda E, j=j, o=o, w=w, slot=slot, ob=ob, sg=sg:
                                  E.tensor_mul(ob[:, j * 512:j * 512 + w],
                                               psB[2 * slot + j][:, 0:w],
                                               sg[:, j * 512:j * 512 + w]),
                                  waits=[(p.SC.h, sv), (sou[slot].h, out_gd[slot])])
                o0 = tcs[ph[0]][0]
                wtot = sum(tcs[t][1] for t in ph)
                out_gd[slot] = p.dma(
                    st, "gpsimd", sou[slot],
                    out_dram_b[m * 128:(m + 1) * 128, o0:o0 + wtot],
                    ob[:, 0:wtot], waits=[(p.VE.h, vv)])
                bank_free_ve[slot] = p.VE.v
                bank_free_sc[slot] = p.SC.v
        mm_after_mg.append(p.MM.v)
    st.flush()



def down_sweep(p, consts, in_dram_b, Kd, w_dram, res_dram, res_off, scale_cm,
               halve, out_dram, Tt, name, out_done_sem=None,
               res2_dram=None, res2_off=0):
    st = Stage(p)
    KT = Kd // 128
    sit = [p.psem(f"ds_i{i}") for i in range(4)]
    swf = [p.psem(f"ds_w{i}") for i in range(2)]
    srs = [p.psem(f"ds_r{i}") for i in range(2)]
    sou = [p.psem(f"ds_o{i}") for i in range(2)]
    sm = p.psem("misc")
    SCOL = None
    if scale_cm is not None:
        SCOL0 = p.sb(f"{name}_sc0", [128, HT], F32)
        SCOL = p.sb(f"{name}_sc", [128, HT], F32)
        lv = p.dma(st, "sync", sm, SCOL0[:], scale_cm)
        p.op(st, "vector", p.VE,
             lambda E: E.tensor_scalar(SCOL[:], SCOL0[:], p.cc(0.5 if halve else 1.0),
                                       None, ALU.mult),
             waits=[(sm.h, lv), (p.PL.h, 1)])
    WF = [p.sb(f"{name}_wf{i}", [128, 512], F32) for i in range(2)]
    WS = [p.sb(f"{name}_ws{k}", [128, 512], BF16) for k in range(KT)]
    INT = [p.sb(f"{name}_it{i}", [128, 1024], BF16) for i in range(4)]
    RES = [p.sb(f"{name}_rs{i}", [128, 512], F32) for i in range(2)]
    RES2 = [p.sb(f"{name}_r2s{i}", [128, 512], F32) for i in range(2)] \
        if res2_dram is not None else None
    sr2 = [p.psem(f"ds_q{i}") for i in range(2)] if res2_dram is not None else None
    OUTF = [p.sb(f"{name}_of{i}", [128, 512], F32) for i in range(2)]
    pss = [p.ps(f"{name}_ps{i}", [128, 512], F32) for i in range(8)]

    tcs = chunks_of(Tt)
    import os as _os
    _tpw = 1 if _os.environ.get('DBG_TP1') else 2
    tps = [tuple(range(i, min(i + _tpw, len(tcs)))) for i in range(0, len(tcs), _tpw)]
    cast_marks = []
    mm_after_dq = []
    evac_ve = {}
    out_gd = [0, 0]
    int_mark = {}

    for dq in range(4):
        prior_mm = mm_after_dq[dq - 1] if dq >= 1 else 0
        cast_marks = []
        for kk in range(KT):
            wf = WF[kk % 2]
            lv = p.dma(st, "sync", swf[kk % 2], wf[:],
                       w_dram[kk * 128:(kk + 1) * 128, dq * 512:(dq + 1) * 512],
                       waits=[(p.PL.h, max(0, p.PL.v - 1))], cls="D")
            cast_marks.append(
                p.op(st, "gpsimd", p.PL,
                     lambda E, wf=wf, kk=kk: E.tensor_copy(WS[kk][:], wf[:]),
                     waits=[(swf[kk % 2].h, lv), (p.MM.h, prior_mm)], cls="D"))
        casts_done = p.PL.v

        for tpi, tp in enumerate(tps):
            o0 = tcs[tp[0]][0]
            wtot = sum(tcs[t][1] for t in tp)
            in_marks = []
            for kk in range(KT):
                it = INT[kk % 4]
                lv = p.dma(st, "sync", sit[kk % 4], it[:, 0:wtot],
                           in_dram_b[kk * 128:(kk + 1) * 128, o0:o0 + wtot],
                           waits=[(p.MM.h, int_mark.get(kk % 4, 0))], cls="A")
                in_marks.append(lv)
                for mi in range(4):
                    for j, tci in enumerate(tp):
                        o, w = tcs[tci]
                        p.op(st, "tensor", p.MM,
                             lambda E, kk=kk, mi=mi, j=j, o=o, w=w, it=it, o0=o0:
                             E.matmul(pss[mi * 2 + j][:, 0:w],
                                      WS[kk][:, mi * 128:(mi + 1) * 128],
                                      it[:, (o - o0):(o - o0) + w],
                                      start=(kk == 0), stop=(kk == KT - 1),
                                      skip_group_check=True),
                             waits=(([(sit[kk % 4].h, lv)] +
                                     ([(p.PL.h, casts_done),
                                       (p.VE.h, evac_ve.get(0, 0))] if kk == 0 else []))
                                    if (mi == 0 and j == 0) else ()), cls="B")
                int_mark[kk % 4] = p.MM.v
            mmv = p.MM.v
            for mi in range(4):
                m = dq * 4 + mi
                for j, tci in enumerate(tp):
                    o, w = tcs[tci]
                    rs = RES[j % 2]
                    of = OUTF[j % 2]
                    rv = p.dma(st, "sync", srs[j % 2], rs[:, 0:w],
                               res_dram[m * 128:(m + 1) * 128,
                                        res_off + o:res_off + o + w],
                               waits=[(p.VE.h, p.VE.v)], cls="E")
                    if res2_dram is not None:
                        rv2 = p.dma(st, "sync", sr2[j % 2], RES2[j % 2][:, 0:w],
                                    res2_dram[m * 128:(m + 1) * 128,
                                              res2_off + o:res2_off + o + w],
                                    waits=[(p.VE.h, p.VE.v)])
                    if scale_cm is not None:
                        vv = p.op(st, "vector", p.VE,
                                  lambda E, mi=mi, j=j, o=o, w=w, m=m, rs=rs, of=of:
                                  E.scalar_tensor_tensor(
                                      of[:, 0:w], pss[mi * 2 + j][:, 0:w],
                                      SCOL[:, m:m + 1], rs[:, 0:w], ALU.mult, ALU.add),
                                  waits=[(p.MM.h, mmv), (srs[j % 2].h, rv),
                                         (sou[j % 2].h, out_gd[j % 2])], cls="C")
                    else:
                        vv = p.op(st, "vector", p.VE,
                                  lambda E, mi=mi, j=j, o=o, w=w, rs=rs, of=of:
                                  E.tensor_add(of[:, 0:w],
                                               pss[mi * 2 + j][:, 0:w],
                                               rs[:, 0:w]),
                                  waits=[(p.MM.h, mmv), (srs[j % 2].h, rv),
                                         (sou[j % 2].h, out_gd[j % 2])], cls="C")
                    if res2_dram is not None:
                        vv = p.op(st, "vector", p.VE,
                                  lambda E, j=j, w=w, of=of: E.tensor_add(
                                      of[:, 0:w], of[:, 0:w], RES2[j % 2][:, 0:w]),
                                  waits=[(sr2[j % 2].h, rv2)])
                    out_gd[j % 2] = p.dma(st, "gpsimd", sou[j % 2],
                                          out_dram[m * 128:(m + 1) * 128, o:o + w],
                                          of[:, 0:w], waits=[(p.VE.h, vv)])
            evac_ve[0] = p.VE.v
        mm_after_dq.append(p.MM.v)
    if out_done_sem is not None:
        p.wait(st, "gpsimd", [(sou[0].h, sou[0].v), (sou[1].h, sou[1].v)])
    st.flush()



def win_sweep(p, consts, z_dram_b, win_dram, convw_cm, convb_cm, mask3_dram,
              u_out_b, sg_out_b, name):
    st = Stage(p)
    sin = p.psem("win_in")
    swf = [p.psem("win_w0"), p.psem("win_w1")]
    sou = p.psem("win_ou")
    sog = [p.psem("win_g0"), p.psem("win_g1")]
    sm = p.psem("misc")
    INB = [p.sb(f"{name}_in{k}", [128, T1], BF16) for k in range(HT)]
    for k in range(HT):
        p.dma(st, "sync", sin, INB[k][:], z_dram_b[k * 128:(k + 1) * 128, :])
    in_all = sin.v
    CW = p.sb(f"{name}_cw", [128, 4 * DIT], F32)
    CB = p.sb(f"{name}_cb", [128, DIT], F32)
    MSK = p.sb(f"{name}_msk", [128, HALO], F32)
    p.dma(st, "sync", sm, CW[:], convw_cm)
    p.dma(st, "sync", sm, CB[:], convb_cm)
    mask_lv = p.dma(st, "sync", sm, MSK[:], mask3_dram)

    MG = 4
    WF = [p.sb(f"{name}_wf{i}", [128, MG * 128], F32) for i in range(2)]
    WS = [[p.sb(f"{name}_ws{pp}_{k}", [128, MG * 128], BF16) for k in range(HT)]
          for pp in range(2)]
    UPRE = p.sb(f"{name}_upre", [128, T1], F32)
    CV = [p.sb(f"{name}_cv{i}", [128, T], F32) for i in range(2)]
    UB = p.sb(f"{name}_ub", [128, T], BF16)
    SGB = [p.sb(f"{name}_sgb{i}", [128, 1024], BF16) for i in range(2)]
    pss = [p.ps(f"{name}_ps{i}", [128, 512], F32) for i in range(8)]

    tcs_u = chunks_of(T1)
    mm_after_mg = []
    cast_done = [0, 0]
    evac_free = {}
    out_gd = {}

    for mg in range(16):
        pp = mg % 2
        u_path = mg < 8
        prior_mm = mm_after_mg[mg - 2] if mg >= 2 else 0
        for kk in range(HT):
            wf = WF[kk % 2]
            lv = p.dma(st, "sync", swf[kk % 2], wf[:],
                       win_dram[kk * 128:(kk + 1) * 128,
                                mg * MG * 128:(mg + 1) * MG * 128],
                       waits=[(p.PL.h, max(0, p.PL.v - 1))])
            p.op(st, "gpsimd", p.PL,
                 lambda E, wf=wf, pp=pp, kk=kk: E.tensor_copy(WS[pp][kk][:], wf[:]),
                 waits=[(swf[kk % 2].h, lv), (p.MM.h, prior_mm)])
        cast_done[pp] = p.PL.v

        for mi in range(MG):
            m = mg * MG + mi
            if u_path:
                d = m
                for ci, (o, w) in enumerate(tcs_u):
                    bank = ci if ci < 4 else 4 + (mi % 4) % 4
                    bank = ci
                    for kk in range(HT):
                        p.op(st, "tensor", p.MM,
                             lambda E, kk=kk, o=o, w=w, bank=bank, pp=pp, mi=mi:
                             E.matmul(pss[bank][:, 0:w],
                                      WS[pp][kk][:, mi * 128:(mi + 1) * 128],
                                      INB[kk][:, o:o + w],
                                      start=(kk == 0), stop=(kk == HT - 1),
                                      skip_group_check=True),
                             waits=([(p.PL.h, cast_done[pp]),
                                     (sin.h, in_all),
                                     (p.SC.h, evac_free.get("u", 0))]
                                    if kk == 0 else ()))
                mmv = p.MM.v
                sv = None
                for ci, (o, w) in enumerate(tcs_u):
                    sv = p.op(st, "scalar", p.SC,
                              lambda E, ci=ci, o=o, w=w: E.copy(UPRE[:, o:o + w],
                                                               pss[ci][:, 0:w]),
                              waits=[(p.MM.h, mmv), (p.VE.h, evac_free.get("uc", 0))])
                evac_free["u"] = p.SC.v
                vv = p.op(st, "vector", p.VE,
                          lambda E: E.tensor_mul(UPRE[:, 0:HALO], UPRE[:, 0:HALO],
                                                 MSK[:]),
                          waits=[(p.SC.h, sv), (sm.h, mask_lv)])
                vv = p.op(st, "vector", p.VE,
                          lambda E, d=d: E.tensor_scalar(
                              CV[0][:], UPRE[:, 0:T], CW[:, 0 * DIT + d:0 * DIT + d + 1],
                              None, ALU.mult),
                          waits=[(p.VE.h, vv)])
                for k in range(1, 4):
                    src, dst = CV[(k + 1) % 2], CV[k % 2]
                    vv = p.op(st, "vector", p.VE,
                              lambda E, k=k, d=d, src=src, dst=dst:
                              E.scalar_tensor_tensor(
                                  dst[:], UPRE[:, k:k + T],
                                  CW[:, k * DIT + d:k * DIT + d + 1], src[:],
                                  ALU.mult, ALU.add))
                evac_free["uc"] = p.VE.v
                sv = p.op(st, "scalar", p.SC,
                          lambda E, d=d: E.activation(UB[:], CV[1][:], AF.Silu,
                                                      bias=CB[:, d:d + 1]),
                          waits=[(p.VE.h, vv), (sou.h, out_gd.get("u", 0))])
                out_gd["u"] = p.dma(st, "gpsimd", sou,
                                    u_out_b[d * 128:(d + 1) * 128, :], UB[:],
                                    waits=[(p.SC.h, sv)])
            else:
                g = m - 32
                for half in range(2):
                    banks = (5, 6)
                    for kk in range(HT):
                        for j in range(2):
                            o = HALO + (half * 2 + j) * 512
                            p.op(st, "tensor", p.MM,
                                 lambda E, kk=kk, j=j, o=o, banks=banks, pp=pp, mi=mi:
                                 E.matmul(pss[banks[j]][:, 0:512],
                                          WS[pp][kk][:, mi * 128:(mi + 1) * 128],
                                          INB[kk][:, o:o + 512],
                                          start=(kk == 0), stop=(kk == HT - 1),
                                          skip_group_check=True),
                                 waits=([(p.PL.h, cast_done[pp]),
                                         (sin.h, in_all),
                                         (p.SC.h, evac_free.get("g", 0))]
                                        if (kk == 0 and j == 0) else ()))
                    mmv = p.MM.v
                    sgb = SGB[half]
                    svs = []
                    for j in range(2):
                        svs.append(p.op(st, "scalar", p.SC,
                                        lambda E, j=j, banks=banks, sgb=sgb:
                                        E.activation(sgb[:, j * 512:(j + 1) * 512],
                                                     pss[banks[j]][:, 0:512], AF.Silu),
                                        waits=[(p.MM.h, mmv),
                                               (sog[half].h, out_gd.get(("g", half), 0))]))
                    evac_free["g"] = p.SC.v
                    out_gd[("g", half)] = p.dma(
                        st, "gpsimd", sog[half],
                        sg_out_b[g * 128:(g + 1) * 128, half * 1024:(half + 1) * 1024],
                        sgb[:], waits=[(p.SC.h, svs[-1])])
        mm_after_mg.append(p.MM.v)
    st.flush()



def ssm_sweep(p, consts, u_dram_b, wx_dram, wdt_dram, dtlnw_cm, bcg_dram,
              bdt_cm, dt_out_f, dtu_out_b, bcn_out_b, name):
    st = Stage(p)
    sm = p.psem("misc")
    swx = p.psem("ssm_wx")
    sut = [p.psem(f"ssm_u{i}") for i in range(4)]
    sum_ = [p.psem(f"ssm_m{i}") for i in range(2)]
    sod = [p.psem(f"ssm_o{i}") for i in range(2)]
    sob = p.psem("ssm_ob")
    DTLN = p.sb(f"{name}_dtln", [128, 1], F32)
    BCG = p.sb(f"{name}_bcg", [32, 1], F32)
    BDT = p.sb(f"{name}_bdt", [128, DIT], F32)
    p.dma(st, "sync", sm, DTLN[:], dtlnw_cm)
    p.dma(st, "sync", sm, BCG[:], bcg_dram)
    misc_all0 = p.dma(st, "sync", sm, BDT[:], bdt_cm)
    WXF = [p.sb(f"{name}_wxf{i}", [128, 160], F32) for i in range(2)]
    WXB = [p.sb(f"{name}_wxb{k}", [128, 160], BF16) for k in range(DIT)]
    wx_casts = []
    for kk in range(DIT):
        wxf = WXF[kk % 2]
        lv = p.dma(st, "sync", swx, wxf[:],
                   wx_dram[kk * 128:(kk + 1) * 128, :],
                   waits=[(p.PL.h, max(0, p.PL.v - 1))])
        wx_casts.append(p.op(st, "gpsimd", p.PL,
                             lambda E, kk=kk, wxf=wxf: E.tensor_copy(WXB[kk][:], wxf[:]),
                             waits=[(swx.h, lv)]))
    WDF = p.sb(f"{name}_wdf", [128, DI], F32)
    WDB = p.sb(f"{name}_wdb", [128, DI], BF16)
    lv = p.dma(st, "sync", sm, WDF[:], wdt_dram)
    wdt_cast = p.op(st, "gpsimd", p.PL,
                    lambda E: E.tensor_copy(WDB[:], WDF[:]),
                    waits=[(sm.h, lv)])

    UT = [p.sb(f"{name}_ut{i}", [128, 512], BF16) for i in range(4)]
    TRS = p.sb(f"{name}_trs", [128, 512], F32)
    TRQ = p.sb(f"{name}_trq", [128, 512], BF16)
    TRN = p.sb(f"{name}_trn", [128, 512], BF16)
    BCS = p.sb(f"{name}_bcs", [32, 512], F32)
    BCQ = p.sb(f"{name}_bcq", [32, 512], BF16)
    BCN = p.sb(f"{name}_bcn", [32, 512], BF16)
    RW = [p.sb(f"{name}_rw{i}", [2, 512], F32) for i in range(2)]
    DTF = [p.sb(f"{name}_dtf{i}", [128, 512], F32) for i in range(2)]
    SP1 = p.sb(f"{name}_sp1", [128, 512], F32)
    SP2 = p.sb(f"{name}_sp2", [128, 512], F32)
    DTU = [p.sb(f"{name}_dtu{i}", [128, 512], BF16) for i in range(2)]
    UM = [p.sb(f"{name}_um{i}", [128, 512], BF16) for i in range(2)]

    trp = p.ps(f"{name}_trp", [128, 512], F32)
    bcp = p.ps(f"{name}_bcp", [32, 512], F32)
    st1 = p.ps(f"{name}_st1", [1, 512], F32)
    st2 = p.ps(f"{name}_st2", [2, 512], F32)
    rb1 = p.ps(f"{name}_rb1", [128, 512], F32)
    rb2 = p.ps(f"{name}_rb2", [32, 512], F32)
    dtp = [p.ps(f"{name}_dtp{i}", [128, 512], F32) for i in range(2)]

    prev_tc_done_ve = 0
    prev_tc_done_sc = 0
    dt_gd = [0, 0]
    bcn_gd = 0
    ut_mark = {}
    for tci, (o, w) in enumerate(chunks_of(T)):
        for kk in range(DIT):
            ut = UT[kk % 4]
            lv = p.dma(st, "sync", sut[kk % 4], ut[:, 0:w],
                       u_dram_b[kk * 128:(kk + 1) * 128, o:o + w],
                       waits=[(p.MM.h, ut_mark.get(kk % 4, 0))])
            p.op(st, "tensor", p.MM,
                 lambda E, kk=kk, w=w, ut=ut: E.matmul(
                     trp[:, 0:w], WXB[kk][:, 0:128], ut[:, 0:w],
                     start=(kk == 0), stop=(kk == DIT - 1), skip_group_check=True),
                 waits=[(sut[kk % 4].h, lv), (p.PL.h, wx_casts[-1]),
                        (p.SC.h, prev_tc_done_sc), (p.VE.h, prev_tc_done_ve)]
                 if kk == 0 else [(sut[kk % 4].h, lv)])
            p.op(st, "tensor", p.MM,
                 lambda E, kk=kk, w=w, ut=ut: E.matmul(
                     bcp[:, 0:w], WXB[kk][:, 128:160], ut[:, 0:w],
                     start=(kk == 0), stop=(kk == DIT - 1), skip_group_check=True))
            ut_mark[kk % 4] = p.MM.v
        mm_proj = p.MM.v

        sv = p.op(st, "scalar", p.SC,
                  lambda E, w=w: (E.copy(TRS[:, 0:w], trp[:, 0:w]),
                                  E.square(TRQ[:, 0:w], trp[:, 0:w]))[-1],
                  waits=[(p.MM.h, mm_proj)])
        mmstat = p.op(st, "tensor", p.MM,
                      lambda E, w=w: E.matmul(st1[:, 0:w], consts["ones_col_b"][:],
                                              TRQ[:, 0:w], start=True, stop=True),
                      waits=[(p.SC.h, sv)])
        sv = p.op(st, "scalar", p.SC,
                  lambda E, w=w: (E.activation(RW[0][0:1, 0:w], st1[:, 0:w], AF.Ln,
                                               scale=p.cr(1.0 / R), bias=p.cr(EPS_RMS)),
                                  E.activation(RW[1][0:1, 0:w], RW[0][0:1, 0:w],
                                               AF.Exp, scale=p.cr(-0.5)))[-1],
                  waits=[(p.MM.h, mmstat)])
        mmb = p.op(st, "tensor", p.MM,
                   lambda E, w=w: E.matmul(rb1[:, 0:w], consts["ones_row_f"][:],
                                           RW[1][0:1, 0:w], start=True, stop=True),
                   waits=[(p.SC.h, sv)])
        vv_trn = p.op(st, "vector", p.VE,
                      lambda E, w=w: E.scalar_tensor_tensor(
                          TRN[:, 0:w], TRS[:, 0:w], DTLN[:], rb1[:, 0:w],
                          ALU.mult, ALU.mult),
                      waits=[(p.MM.h, mmb)])

        sv = p.op(st, "scalar", p.SC,
                  lambda E, w=w: (E.copy(BCS[:, 0:w], bcp[:, 0:w]),
                                  E.square(BCQ[:, 0:w], bcp[:, 0:w]))[-1],
                  waits=[(p.MM.h, mm_proj)])
        mmstat = p.op(st, "tensor", p.MM,
                      lambda E, w=w: E.matmul(st2[:, 0:w], consts["selB_b"][:],
                                              BCQ[:, 0:w], start=True, stop=True),
                      waits=[(p.SC.h, sv)])
        sv = p.op(st, "scalar", p.SC,
                  lambda E, w=w: (E.activation(RW[0][0:2, 0:w], st2[:, 0:w], AF.Ln,
                                               scale=p.crn(1.0 / NST, 2),
                                               bias=p.crn(EPS_RMS, 2)),
                                  E.activation(RW[1][0:2, 0:w], RW[0][0:2, 0:w],
                                               AF.Exp, scale=p.crn(-0.5, 2)))[-1],
                  waits=[(p.MM.h, mmstat)])
        mmb = p.op(st, "tensor", p.MM,
                   lambda E, w=w: E.matmul(rb2[:, 0:w], consts["selBC2_f"][:],
                                           RW[1][0:2, 0:w], start=True, stop=True),
                   waits=[(p.SC.h, sv)])
        vv = p.op(st, "vector", p.VE,
                  lambda E, w=w: E.scalar_tensor_tensor(
                      BCN[:, 0:w], BCS[:, 0:w], BCG[:], rb2[:, 0:w],
                      ALU.mult, ALU.mult),
                  waits=[(p.MM.h, mmb), (sob.h, bcn_gd)])
        bcn_gd = p.dma(st, "gpsimd", sob, bcn_out_b[:, o:o + w], BCN[:, 0:w],
                       waits=[(p.VE.h, vv)])

        for m in range(DIT):
            sl = m % 2
            mmv = p.op(st, "tensor", p.MM,
                       lambda E, m=m, w=w, sl=sl: E.matmul(
                           dtp[sl][:, 0:w], WDB[:, m * 128:(m + 1) * 128],
                           TRN[:, 0:w], start=True, stop=True, skip_group_check=True),
                       waits=[(p.VE.h, vv_trn), (p.PL.h, wdt_cast),
                              (p.SC.h, prev_tc_done_sc)] if m == 0 else
                       [(p.SC.h, (p.SC.v - 1) if m >= 2 else 0)])
            lvu = p.dma(st, "sync", sum_[sl], UM[sl][:, 0:w],
                        u_dram_b[m * 128:(m + 1) * 128, o:o + w],
                        waits=[(p.VE.h, max(0, p.VE.v - 1))])
            sv = p.op(st, "scalar", p.SC,
                      lambda E, m=m, w=w, sl=sl:
                      (E.activation(SP1[:, 0:w], dtp[sl][:, 0:w], AF.Abs,
                                    bias=BDT[:, m:m + 1]),
                       E.activation(SP1[:, 0:w], SP1[:, 0:w], AF.Exp,
                                    scale=p.cc(-1.0)),
                       E.activation(SP1[:, 0:w], SP1[:, 0:w], AF.Ln,
                                    bias=p.cc(1.0)),
                       E.activation(SP2[:, 0:w], dtp[sl][:, 0:w], AF.Relu,
                                    bias=BDT[:, m:m + 1]))[-1],
                      waits=[(p.MM.h, mmv), (p.VE.h, max(0, p.VE.v)),
                             (sm.h, misc_all0)])
            vv = p.op(st, "vector", p.VE,
                      lambda E, m=m, w=w, sl=sl: E.tensor_add(
                          DTF[sl][:, 0:w], SP1[:, 0:w], SP2[:, 0:w]),
                      waits=[(p.SC.h, sv), (sod[sl].h, dt_gd[sl])])
            vv = p.op(st, "vector", p.VE,
                      lambda E, m=m, w=w, sl=sl: E.tensor_mul(
                          DTU[sl][:, 0:w], DTF[sl][:, 0:w], UM[sl][:, 0:w]),
                      waits=[(sum_[sl].h, lvu)])
            p.dma(st, "gpsimd", sod[sl], dt_out_f[m * 128:(m + 1) * 128, o:o + w],
                  DTF[sl][:, 0:w], waits=[(p.VE.h, vv)])
            dt_gd[sl] = p.dma(st, "gpsimd", sod[sl],
                              dtu_out_b[m * 128:(m + 1) * 128, o:o + w],
                              DTU[sl][:, 0:w])
        prev_tc_done_ve = p.VE.v
        prev_tc_done_sc = p.SC.v
    st.flush()



def scan_sweep(p, consts, mode, dt_dram_f, dtu_dram_b, bcn_dram_b, alog_dram,
               name, summ_out=None, h0_dram=None, u_dram_b=None, dskip_cm=None,
               sg_dram_b=None, yg_out_b=None):
    st = Stage(p)
    sm = p.psem("misc")
    stl = [p.psem(f"sc_t{i}") for i in range(2)]
    sbc = p.psem("sc_bc")
    syo = [p.psem(f"sc_y{i}") for i in range(2)]
    sso = p.psem("sc_s")
    TC = 1024
    NTC = T // TC
    AALL = p.sb(f"{name}_aall", [128, 512], F32)
    GALL = p.sb(f"{name}_gall", [128, 512], F32)
    STATE = p.sb(f"{name}_state", [128, 512], F32)
    SDT = p.sb(f"{name}_sdt", [128, DIT], F32)
    SDTT = p.sb(f"{name}_sdtt", [128, 2 * DIT], F32)
    BRE = [p.sb(f"{name}_bre{n}", [128, TC], BF16) for n in range(NST)]
    CRE = [p.sb(f"{name}_cre{n}", [128, TC], BF16) for n in range(NST)]
    DTT = [p.sb(f"{name}_dtt{i}", [128, TC], F32) for i in range(2)]
    DUT = [p.sb(f"{name}_dut{i}", [128, TC], BF16) for i in range(2)]
    DA = [p.sb(f"{name}_da{i}", [128, TC], F32) for i in range(2)]
    XB = [p.sb(f"{name}_xb{i}", [128, TC], BF16) for i in range(2)]
    HB = [p.sb(f"{name}_hb{i}", [128, TC], BF16) for i in range(2)]
    if mode == "full":
        UT = [p.sb(f"{name}_ut{i}", [128, TC], BF16) for i in range(2)]
        SGT = [p.sb(f"{name}_sgt{i}", [128, TC], BF16) for i in range(2)]
        YF = [p.sb(f"{name}_yf{i}", [128, TC], F32) for i in range(2)]
        YG = [p.sb(f"{name}_yg{i}", [128, TC], BF16) for i in range(2)]
        ACC = [p.sb(f"{name}_acc{i}", [128, TC], BF16) for i in range(4)]
        DSK = p.sb(f"{name}_dsk", [128, DIT], F32)
        H0 = p.sb(f"{name}_h0", [128, 512], F32)
        p.dma(st, "sync", sm, DSK[:], dskip_cm)
        h0lv = p.dma(st, "sync", sm, H0[:], h0_dram)

    for dtile in range(DIT):
        p.dma(st, "sync", sm, AALL[:, dtile * 16:(dtile + 1) * 16],
              alog_dram[dtile * 128:(dtile + 1) * 128, :])
    sva = p.op(st, "scalar", p.SC,
               lambda E: E.activation(AALL[:], AALL[:], AF.Exp),
               waits=[(sm.h, sm.v)])
    vva = p.op(st, "vector", p.VE,
               lambda E: (E.tensor_scalar(AALL[:], AALL[:], p.cc(-1.0), None, ALU.mult),
                          E.memset(STATE[:], 0.0))[-1],
               waits=[(p.SC.h, sva), (p.PL.h, 1)])

    scan_ready_ve = vva
    yg_gd = [0, 0]
    for tci in range(NTC):
        o = tci * TC
        for n in range(NST):
            src_b = bass.AP(bcn_dram_b.tensor, bcn_dram_b.offset + n * T + o,
                            [[0, 128], [1, TC]])
            src_c = bass.AP(bcn_dram_b.tensor, bcn_dram_b.offset + (16 + n) * T + o,
                            [[0, 128], [1, TC]])
            p.dma(st, "gpsimd", sbc, BRE[n][:], src_b,
                  waits=[(p.VE.h, scan_ready_ve)])
            p.dma(st, "gpsimd", sbc, CRE[n][:], src_c)
        bc_lv = sbc.v

        for dtile in range(DIT):
            sl = dtile % 2
            war_ve = p.VE.v
            p.dma(st, "sync", stl[sl], DTT[sl][:],
                  dt_dram_f[dtile * 128:(dtile + 1) * 128, o:o + TC],
                  waits=[(p.VE.h, war_ve), (p.SC.h, p.SC.v)])
            p.dma(st, "sync", stl[sl], DUT[sl][:],
                  dtu_dram_b[dtile * 128:(dtile + 1) * 128, o:o + TC])
            if mode == "full":
                p.dma(st, "sync", stl[sl], UT[sl][:],
                      u_dram_b[dtile * 128:(dtile + 1) * 128, o:o + TC])
                p.dma(st, "sync", stl[sl], SGT[sl][:],
                      sg_dram_b[dtile * 128:(dtile + 1) * 128, o:o + TC])
            lv_dt = lv_du = lv_u = lv_sg = stl[sl].v
            if mode == "full":
                vv_y = p.op(st, "vector", p.VE,
                            lambda E, sl=sl, dtile=dtile: E.tensor_scalar(
                                YF[sl][:], UT[sl][:], DSK[:, dtile:dtile + 1],
                                None, ALU.mult),
                            waits=[(stl[sl].h, lv_u)])
            tts_after = {0: 0, 1: 0}
            for n in range(NST):
                col = dtile * 16 + n
                hsl = n % 2
                sv = p.op(st, "scalar", p.SC,
                          lambda E, sl=sl, hsl=hsl, col=col: E.activation(
                              DA[hsl][:], DTT[sl][:], AF.Exp,
                              scale=AALL[:, col:col + 1]),
                          waits=[(stl[sl].h, lv_dt), (p.VE.h, tts_after[hsl])])
                p.op(st, "vector", p.VE,
                     lambda E, sl=sl, n=n: E.tensor_mul(
                         XB[sl][:], DUT[sl][:], BRE[n][:]),
                     waits=[(stl[sl].h, lv_du), (sbc.h, bc_lv)])
                if mode == "full" and tci == 0:
                    init_src = H0
                    if dtile == 0 and n == 0:
                        p.wait(st, "vector", [(sm.h, h0lv)])
                else:
                    init_src = STATE
                p.op(st, "vector", p.VE,
                     lambda E, sl=sl, hsl=hsl, col=col, init_src=init_src:
                     E.tensor_tensor_scan(HB[hsl][:], DA[hsl][:], XB[sl][:],
                                          init_src[:, col:col + 1],
                                          ALU.mult, ALU.add),
                     waits=[(p.SC.h, sv)])
                tts_after[hsl] = p.VE.v
                p.op(st, "scalar", p.SC,
                     lambda E, hsl=hsl, col=col: E.copy(
                         STATE[:, col:col + 1], HB[hsl][:, TC - 1:TC]),
                     waits=[(p.VE.h, tts_after[hsl])])
                if mode == "full":
                    if n < 4:
                        p.op(st, "vector", p.VE,
                             lambda E, hsl=hsl, n=n: E.tensor_mul(
                                 ACC[n][:], HB[hsl][:], CRE[n][:]))
                    else:
                        p.op(st, "vector", p.VE,
                             lambda E, hsl=hsl, n=n: E.tensor_mul(
                                 HB[hsl][:], HB[hsl][:], CRE[n][:]))
                        p.op(st, "vector", p.VE,
                             lambda E, hsl=hsl, n=n: E.tensor_add(
                                 ACC[n % 4][:], ACC[n % 4][:], HB[hsl][:]))
            if mode == "local":
                p.op(st, "vector", p.VE,
                     lambda E, sl=sl, dtile=dtile, tci=tci: E.reduce_sum(
                         SDTT[:, tci * DIT + dtile:tci * DIT + dtile + 1],
                         DTT[sl][:], axis=AX.X))
            else:
                p.op(st, "vector", p.VE,
                     lambda E: E.tensor_add(ACC[0][:], ACC[0][:], ACC[1][:]))
                p.op(st, "vector", p.VE,
                     lambda E: E.tensor_add(ACC[2][:], ACC[2][:], ACC[3][:]))
                p.op(st, "vector", p.VE,
                     lambda E: E.tensor_add(ACC[0][:], ACC[0][:], ACC[2][:]))
                p.op(st, "vector", p.VE,
                     lambda E, sl=sl: E.tensor_add(YF[sl][:], YF[sl][:], ACC[0][:]))
                vv = p.op(st, "vector", p.VE,
                          lambda E, sl=sl: E.tensor_mul(YG[sl][:], YF[sl][:],
                                                        SGT[sl][:]),
                          waits=[(stl[sl].h, lv_sg), (syo[sl].h, yg_gd[sl])])
                yg_gd[sl] = p.dma(st, "gpsimd", syo[sl],
                                  yg_out_b[dtile * 128:(dtile + 1) * 128, o:o + TC],
                                  YG[sl][:], waits=[(p.VE.h, vv)])
        scan_ready_ve = p.VE.v

    if mode == "local":
        vvs = p.op(st, "vector", p.VE,
                   lambda E: E.tensor_add(SDT[:], SDTT[:, 0:DIT],
                                          SDTT[:, DIT:2 * DIT]),
                   waits=[(p.VE.h, p.VE.v)])
        svs = []
        for dtile in range(DIT):
            svs.append(p.op(st, "scalar", p.SC,
                            lambda E, dtile=dtile: E.activation(
                                GALL[:, dtile * 16:(dtile + 1) * 16],
                                AALL[:, dtile * 16:(dtile + 1) * 16], AF.Exp,
                                scale=SDT[:, dtile:dtile + 1]),
                            waits=[(p.VE.h, vvs)] if dtile == 0 else ()))
        p.dma(st, "gpsimd", sso, summ_out[0], STATE[:],
              waits=[(p.VE.h, p.VE.v)])
        p.dma(st, "gpsimd", sso, summ_out[1], GALL[:],
              waits=[(p.SC.h, svs[-1])])
        p.wait(st, "gpsimd", [(sso.h, sso.v)])
    st.flush()


def combine_chain(p, consts, psm_dram, h0_out_dram, name):
    st = Stage(p)
    GT = [p.sb(f"{name}_gt{j}", [128, 512], F32) for j in range(3)]
    ET = [p.sb(f"{name}_et{j}", [128, 512], F32) for j in range(3)]
    H0 = p.sb(f"{name}_h0", [128, 512], F32)
    sm = p.psem("misc")
    for j in range(3):
        p.dma(st, "sync", sm, ET[j][:], psm_dram[j, 0])
        p.dma(st, "sync", sm, GT[j][:], psm_dram[j, 1])
    lv_all = sm.v

    def f(E):
        E.memset(H0[:], 0.0)
        ins = None
        for j in range(3):
            E.tensor_mul(H0[:], H0[:], GT[j][:])
            ins = E.tensor_add(H0[:], H0[:], ET[j][:])
        return ins

    vv = p.op(st, "vector", p.VE, f, waits=[(sm.h, lv_all)])
    sso = p.psem("sc_s")
    p.dma(st, "gpsimd", sso, h0_out_dram, H0[:], waits=[(p.VE.h, vv)])
    p.wait(st, "gpsimd", [(sso.h, sso.v)])
    st.flush()



L1_STAGES = ("ln1", "ff1", "x1", "z", "win", "ssm", "scan")
L2_STAGES = ("comb", "scan", "x2", "h2", "ffi", "x3", "ln3", "ff3", "out")


def build_l1(upto="scan", dbg=()):
    nc = bass.Bass("TRN2", target_bir_lowering=False, debug=False)
    DT = nc.dram_tensor

    def kind_of(name):
        return "ExternalOutput" if name in dbg else "Internal"

    x_t = DT("x_t", [H, T1], F32, kind="ExternalInput").ap()
    gw_mac = DT("gw_mac", [H, FF], F32, kind="ExternalInput").ap()
    uw_mac = DT("uw_mac", [H, FF], F32, kind="ExternalInput").ap()
    dw_mac = DT("dw_mac", [FF, H], F32, kind="ExternalInput").ap()
    gs_cm = DT("gs_cm", [128, FFT], F32, kind="ExternalInput").ap()
    us_cm = DT("us_cm", [128, FFT], F32, kind="ExternalInput").ap()
    ds_cm = DT("ds_cm", [128, HT], F32, kind="ExternalInput").ap()
    ilnw_cm = DT("ilnw_cm", [128, HT], F32, kind="ExternalInput").ap()
    win_t = DT("win_t", [H, 2 * DI], F32, kind="ExternalInput").ap()
    convw_cm = DT("convw_cm", [128, 4 * DIT], F32, kind="ExternalInput").ap()
    convb_cm = DT("convb_cm", [128, DIT], F32, kind="ExternalInput").ap()
    mask3 = DT("mask3", [128, HALO], F32, kind="ExternalInput").ap()
    wx_t = DT("wx_t", [DI, 160], F32, kind="ExternalInput").ap()
    dtlnw_cm = DT("dtlnw_cm", [128, 1], F32, kind="ExternalInput").ap()
    bcg = DT("bcg", [32, 1], F32, kind="ExternalInput").ap()
    wdt_t = DT("wdt_t", [R, DI], F32, kind="ExternalInput").ap()
    bdt_cm = DT("bdt_cm", [128, DIT], F32, kind="ExternalInput").ap()
    alog = DT("alog", [DI, NST], F32, kind="ExternalInput").ap()
    selB_d = DT("selB_d", [32, 2], BF16, kind="ExternalInput").ap()
    selBC2_d = DT("selBC2_d", [2, 32], F32, kind="ExternalInput").ap()

    summ = DT("summ", [2, 128, 512], F32, kind="ExternalOutput").ap()
    dt_f = DT("dt_f", [DI, T], F32, kind="ExternalOutput").ap()
    dtu_b = DT("dtu_b", [DI, T], BF16, kind="ExternalOutput").ap()
    u_b = DT("u_b", [DI, T], BF16, kind="ExternalOutput").ap()
    sg_b = DT("sg_b", [DI, T], BF16, kind="ExternalOutput").ap()
    x1_f = DT("x1_f", [H, T1], F32, kind="ExternalOutput").ap()
    ln2_f = DT("ln2_f", [H, T1], F32, kind="ExternalOutput").ap()
    bcn_b = DT("bcn_b", [32, T], BF16, kind="ExternalOutput").ap()

    ln1_b = DT("ln1_b", [H, T1], BF16, kind=kind_of("ln1_b")).ap()
    ff1_b = DT("ff1_b", [FF, T1], BF16, kind=kind_of("ff1_b")).ap()
    z_b = DT("z_b", [H, T1], BF16, kind=kind_of("z_b")).ap()

    with ExitStack() as ctx:
        block = ctx.enter_context(nc.Block())
        p = Prog(nc, ctx, block)
        consts = init_consts(p, selB_d, selBC2_d)
        order = []

        def upto_done():
            return upto in order

        order.append("ln1")
        with p.scope():
            ln_sweep(p, consts, "ln", x_t, ln1_b, None, T1, "s1")
        if not upto_done():
            order.append("ff1")
            with p.scope():
                gu_sweep(p, consts, ln1_b, gw_mac, uw_mac, gs_cm, us_cm, ff1_b,
                         T1, "s2")
        if not upto_done():
            order.append("x1")
            with p.scope():
                down_sweep(p, consts, ff1_b, FF, dw_mac, x_t, 0, ds_cm, True,
                           x1_f, T1, "s3")
        if not upto_done():
            order.append("z")
            with p.scope():
                ln_sweep(p, consts, "lnrms", x1_f, z_b, ilnw_cm, T1, "s4",
                         out2_dram=ln2_f)
        if not upto_done():
            order.append("win")
            with p.scope():
                win_sweep(p, consts, z_b, win_t, convw_cm, convb_cm, mask3,
                          u_b, sg_b, "s5")
        if not upto_done():
            order.append("ssm")
            with p.scope():
                ssm_sweep(p, consts, u_b, wx_t, wdt_t, dtlnw_cm, bcg, bdt_cm,
                          dt_f, dtu_b, bcn_b, "s6")
        if not upto_done():
            order.append("scan")
            with p.scope():
                scan_sweep(p, consts, "local", dt_f, dtu_b, bcn_b, alog, "s7",
                           summ_out=summ)
        st = Stage(p)
        p.wait(st, "gpsimd", [(s.h, s.v) for s in p.all_sems])
        st.flush()
    return nc


def build_l2(upto="out", dbg=()):
    nc = bass.Bass("TRN2", target_bir_lowering=False, debug=False)
    DT = nc.dram_tensor

    def kind_of(name):
        return "ExternalOutput" if name in dbg else "Internal"

    dt_f = DT("dt_f", [DI, T], F32, kind="ExternalInput").ap()
    dtu_b = DT("dtu_b", [DI, T], BF16, kind="ExternalInput").ap()
    u_b = DT("u_b", [DI, T], BF16, kind="ExternalInput").ap()
    sg_b = DT("sg_b", [DI, T], BF16, kind="ExternalInput").ap()
    x1_f = DT("x1_f", [H, T1], F32, kind="ExternalInput").ap()
    ln2_f = DT("ln2_f", [H, T1], F32, kind="ExternalInput").ap()
    bcn_b = DT("bcn_b", [32, T], BF16, kind="ExternalInput").ap()
    psm = DT("psm", [3, 2, 128, 512], F32, kind="ExternalInput").ap()
    alog = DT("alog", [DI, NST], F32, kind="ExternalInput").ap()
    dskip_cm = DT("dskip_cm", [128, DIT], F32, kind="ExternalInput").ap()
    wout_t = DT("wout_t", [DI, H], F32, kind="ExternalInput").ap()
    pfw_cm = DT("pfw_cm", [128, HT], F32, kind="ExternalInput").ap()
    gw_att = DT("gw_att", [H, FF], F32, kind="ExternalInput").ap()
    uw_att = DT("uw_att", [H, FF], F32, kind="ExternalInput").ap()
    dw_att = DT("dw_att", [FF, H], F32, kind="ExternalInput").ap()
    gw_mac = DT("gw_mac", [H, FF], F32, kind="ExternalInput").ap()
    uw_mac = DT("uw_mac", [H, FF], F32, kind="ExternalInput").ap()
    dw_mac = DT("dw_mac", [FF, H], F32, kind="ExternalInput").ap()
    gs_cm = DT("gs_cm", [128, FFT], F32, kind="ExternalInput").ap()
    us_cm = DT("us_cm", [128, FFT], F32, kind="ExternalInput").ap()
    ds_cm = DT("ds_cm", [128, HT], F32, kind="ExternalInput").ap()

    out_t = DT("out_t", [H, T], F32, kind="ExternalOutput").ap()

    h0_d = DT("h0_d", [128, 512], F32, kind=kind_of("h0_d")).ap()
    yg_b = DT("yg_b", [DI, T], BF16, kind=kind_of("yg_b")).ap()
    x2_f = DT("x2_f", [H, T], F32, kind=kind_of("x2_f")).ap()
    h2_b = DT("h2_b", [H, T], BF16, kind=kind_of("h2_b")).ap()
    ffi_b = DT("ffi_b", [FF, T], BF16, kind=kind_of("ffi_b")).ap()
    x3_f = DT("x3_f", [H, T], F32, kind=kind_of("x3_f")).ap()
    ln3_b = DT("ln3_b", [H, T], BF16, kind=kind_of("ln3_b")).ap()
    ff3_b = DT("ff3_b", [FF, T], BF16, kind=kind_of("ff3_b")).ap()

    with ExitStack() as ctx:
        block = ctx.enter_context(nc.Block())
        p = Prog(nc, ctx, block)
        consts = init_consts(p)
        order = []

        def upto_done():
            return upto in order

        order.append("comb")
        with p.scope():
            combine_chain(p, consts, psm, h0_d, "c0")
        if not upto_done():
            order.append("scan")
            with p.scope():
                scan_sweep(p, consts, "full", dt_f, dtu_b, bcn_b, alog, "s7",
                           h0_dram=h0_d, u_dram_b=u_b, dskip_cm=dskip_cm,
                           sg_dram_b=sg_b, yg_out_b=yg_b)
        if not upto_done():
            order.append("x2")
            with p.scope():
                down_sweep(p, consts, yg_b, DI, wout_t, ln2_f, HALO, None, False,
                           x2_f, T, "s9")
        if not upto_done():
            order.append("h2")
            with p.scope():
                ln_sweep(p, consts, "rms", x2_f, h2_b, pfw_cm, T, "s9b")
        if not upto_done():
            order.append("ffi")
            with p.scope():
                gu_sweep(p, consts, h2_b, gw_att, uw_att, None, None, ffi_b, T, "s10")
        if not upto_done():
            order.append("x3")
            with p.scope():
                down_sweep(p, consts, ffi_b, FF, dw_att, x2_f, 0, None, False,
                           x3_f, T, "s11", res2_dram=x1_f, res2_off=HALO)
        if not upto_done():
            order.append("ln3")
            with p.scope():
                ln_sweep(p, consts, "ln", x3_f, ln3_b, None, T, "s12")
        if not upto_done():
            order.append("ff3")
            with p.scope():
                gu_sweep(p, consts, ln3_b, gw_mac, uw_mac, gs_cm, us_cm, ff3_b,
                         T, "s13")
        if not upto_done():
            order.append("out")
            with p.scope():
                down_sweep(p, consts, ff3_b, FF, dw_mac, x3_f, 0, ds_cm, True,
                           out_t, T, "s14")
        st = Stage(p)
        p.wait(st, "gpsimd", [(s.h, s.v) for s in p.all_sems])
        st.flush()
    return nc



def _cm(v):
    v = np.ascontiguousarray(np.asarray(v, dtype=np.float32))
    return np.ascontiguousarray(v.reshape(-1, 128).T)


def _wT(w):
    return np.ascontiguousarray(np.asarray(w, dtype=np.float32).T)


def host_prep(inputs):
    g = {}
    g["gw_mac"] = _wT(inputs["L0_ffn_gate_w"])
    g["uw_mac"] = _wT(inputs["L0_ffn_up_w"])
    g["dw_mac"] = _wT(inputs["L0_ffn_down_w"])
    g["gs_cm"] = _cm(inputs["L0_ffn_gate_s"])
    g["us_cm"] = _cm(inputs["L0_ffn_up_s"])
    g["ds_cm"] = _cm(inputs["L0_ffn_down_s"])
    g["ilnw_cm"] = _cm(inputs["L0_attn_input_ln_w"])
    g["win_t"] = _wT(inputs["L0_attn_w_in"])
    cw = np.asarray(inputs["L0_attn_conv_w"], np.float32).reshape(DI, 4)
    g["convw_cm"] = np.ascontiguousarray(
        np.hstack([_cm(cw[:, k]) for k in range(4)]))
    g["convb_cm"] = _cm(inputs["L0_attn_conv_b"])
    g["wx_t"] = _wT(inputs["L0_attn_w_x"])
    g["dtlnw_cm"] = _cm(inputs["L0_attn_dt_ln_w"])
    g["bcg"] = np.ascontiguousarray(np.concatenate(
        [np.asarray(inputs["L0_attn_b_ln_w"], np.float32),
         np.asarray(inputs["L0_attn_c_ln_w"], np.float32)]).reshape(32, 1))
    g["wdt_t"] = _wT(inputs["L0_attn_w_dt"])
    g["bdt_cm"] = _cm(inputs["L0_attn_b_dt"])
    g["alog"] = np.ascontiguousarray(np.asarray(inputs["L0_attn_A_log"], np.float32))
    g["dskip_cm"] = _cm(inputs["L0_attn_D_skip"])
    g["wout_t"] = _wT(inputs["L0_attn_w_out"])
    g["pfw_cm"] = _cm(inputs["L0_attn_pre_ff_ln_w"])
    g["gw_att"] = _wT(inputs["L0_attn_w_gate"])
    g["uw_att"] = _wT(inputs["L0_attn_w_up"])
    g["dw_att"] = _wT(inputs["L0_attn_w_down"])
    selB = np.zeros((32, 2), np.float32)
    selB[0:16, 0] = 1.0
    selB[16:32, 1] = 1.0
    g["selB_d"] = selB.astype(NPBF16)
    selBC2 = np.zeros((2, 32), np.float32)
    selBC2[0, 0:16] = 1.0
    selBC2[1, 16:32] = 1.0
    g["selBC2_d"] = np.ascontiguousarray(selBC2)

    x = np.asarray(inputs["L0_attn_hidden_states"], np.float32)
    shards = []
    masks = []
    for r in range(NCORES):
        b, c = r // 4, r % 4
        s0 = c * T
        xs = np.zeros((H, T1), np.float32)
        if c == 0:
            xs[:, HALO:] = x[b, s0:s0 + T, :].T
            masks.append(np.zeros((128, HALO), np.float32))
        else:
            xs[:, :] = x[b, s0 - HALO:s0 + T, :].T
            masks.append(np.ones((128, HALO), np.float32))
        shards.append(np.ascontiguousarray(xs))
    g["x_shards"] = shards
    g["masks"] = masks
    return g


L1_SHARED = ["gw_mac", "uw_mac", "dw_mac", "gs_cm", "us_cm", "ds_cm",
             "ilnw_cm", "win_t", "convw_cm", "convb_cm", "wx_t", "dtlnw_cm",
             "bcg", "wdt_t", "bdt_cm", "alog", "selB_d", "selBC2_d"]
L2_SHARED = ["alog", "dskip_cm", "wout_t", "pfw_cm", "gw_att", "uw_att",
             "dw_att", "gw_mac", "uw_mac", "dw_mac", "gs_cm", "us_cm", "ds_cm"]
L1_CARRY = ["dt_f", "dtu_b", "u_b", "sg_b", "x1_f", "ln2_f", "bcn_b"]


def make_psm(res1, r):
    b, c = r // 4, r % 4
    slots = []
    for j in range(3):
        k = j - (3 - c)
        if k < 0:
            slots.append(np.stack([np.zeros((128, 512), np.float32),
                                   np.ones((128, 512), np.float32)]))
        else:
            slots.append(np.asarray(res1[4 * b + k]["summ"], np.float32))
    return np.ascontiguousarray(np.stack(slots))


def kernel(**inputs):
    g = host_prep(inputs)
    nc1 = build_l1()
    in1 = []
    for r in range(NCORES):
        m = {k: g[k] for k in L1_SHARED}
        m["x_t"] = g["x_shards"][r]
        m["mask3"] = g["masks"][r]
        in1.append(m)
    r1 = run_bass_kernel_spmd(nc1, in1, core_ids=list(range(NCORES)))
    res1 = r1.results

    nc2 = build_l2()
    in2 = []
    for r in range(NCORES):
        m = {k: g[k] for k in L2_SHARED}
        for k in L1_CARRY:
            m[k] = np.asarray(res1[r][k])
        m["psm"] = make_psm(res1, r)
        in2.append(m)
    r2 = run_bass_kernel_spmd(nc2, in2, core_ids=list(range(NCORES)))

    out = np.empty((B, S, H), np.float32)
    for r in range(NCORES):
        b, c = r // 4, r % 4
        out[b, c * T:(c + 1) * T, :] = np.asarray(r2.results[r]["out_t"]).T
    return out
